# version 06
# baseline (speedup 1.0000x reference)
import numpy as np
from contextlib import ExitStack

import concourse.bass as bass
import concourse.mybir as mybir
from concourse.alu_op_type import AluOpType
from concourse.bass_utils import run_bass_kernel_spmd

F32 = mybir.dt.float32
F16 = mybir.dt.float16
AF = mybir.ActivationFunctionType
AX = mybir.AxisListType.X

C = 2048
NCHUNK = C // 128
TPASS = 512
NPASS = 2
ROWS_PER_CORE = TPASS * NPASS
ICHUNK = 8
NEXP = 16
EPS = 1e-5
SPARSE = True
CAP = 256
NSC = CAP // 128

ENGINES = ("sync", "scalar", "vector", "tensor", "gpsimd")


class Prog:

    DMA_SEMS = ("dma", "w0", "w1", "out", "gd")

    def __init__(self):
        self.streams = {e: [] for e in ENGINES}
        self.count = {e: 0 for e in ENGINES}
        self.dmac = {s: 0 for s in self.DMA_SEMS}
        self.barrier_toks = []
        self.xfence = None
        self.qfence = []

    def op(self, eng, fn, after=(), inc=0):
        deps = list(self.barrier_toks)
        for t in after:
            if t is None:
                continue
            if isinstance(t, list):
                deps += [u for u in t if u is not None]
            else:
                deps.append(t)
        self.streams[eng].append((deps, fn, inc))
        if inc:
            self.count[eng] += inc
            return (eng, self.count[eng])
        return None

    def dma_op(self, fn, after=(), sem="dma", eng="sync", fence=False):
        deps = list(self.barrier_toks) + list(self.qfence)
        for t in after:
            if t is None:
                continue
            if isinstance(t, list):
                deps += [u for u in t if u is not None]
            else:
                deps.append(t)
        if self.xfence is not None:
            deps.append(self.xfence)
        self.dmac[sem] += 16
        self.streams[eng].append((deps, fn, (sem, 16)))
        tok = (sem, self.dmac[sem])
        if fence:
            self.qfence = [tok]
        return tok

    def dma_totals(self):
        return [(s, v) for s, v in self.dmac.items() if v > 0]

    def barrier(self):
        toks = [(e, self.count[e]) for e in ENGINES if self.count[e] > 0]
        toks += [(s, v) for s, v in self.dmac.items() if v > 0]
        self.barrier_toks = toks

    def emit(self, eng_name, eng, sems):
        waited = {}
        need_drain = eng_name in ("vector", "scalar", "gpsimd")
        first = True
        for deps, fn, inc in self.streams[eng_name]:
            for dep_eng, val in deps:
                if dep_eng == eng_name:
                    continue
                if waited.get(dep_eng, 0) < val:
                    eng.wait_ge(sems[dep_eng], val)
                    waited[dep_eng] = val
            if need_drain and not first and not isinstance(inc, tuple):
                eng.drain()
            first = False
            inst = fn(eng)
            if inc:
                if isinstance(inc, tuple):
                    inst.then_inc(sems[inc[0]], inc[1])
                else:
                    inst.then_inc(sems[eng_name], inc)


class WStream:

    N_RAW = 2
    N_HL = 4

    def __init__(self, p, WRAW, WHI, WLO):
        self.p, self.WRAW, self.WHI, self.WLO = p, WRAW, WHI, WLO
        self.i = 0
        self.split_tok = [None] * self.N_RAW
        self.pe_tok = [None] * self.N_HL

    def stage(self, dram_ap, width=2048):
        p = self.p
        r = self.i % self.N_RAW
        s = self.i % self.N_HL
        self.i += 1
        raw = self.WRAW[:, r * 2048: r * 2048 + width]
        hi = self.WHI[:, s * 2048: s * 2048 + width]
        lo = self.WLO[:, s * 2048: s * 2048 + width]
        dma_tok = p.dma_op(lambda e, o=raw, i_=dram_ap: e.dma_start(out=o, in_=i_),
                           after=[self.split_tok[r]], sem=f"w{r}")
        hi_tok = p.op("scalar", lambda e, o=hi, i_=raw: e.copy(o, i_),
                      after=[dma_tok, self.pe_tok[s]], inc=1)
        lo_tok = p.op("vector",
                      lambda e, o=lo, a=raw, b=hi: e.tensor_tensor(o, a, b, AluOpType.subtract),
                      after=[dma_tok, self.pe_tok[s], hi_tok], inc=1)
        self.split_tok[r] = lo_tok
        return s, hi, lo, [hi_tok, lo_tok]

    def consumed(self, s, pe_tok):
        self.pe_tok[s] = pe_tok


def build_nc(n_pass=NPASS, attn_only=False):
    nc = bass.Bass("TRN2", target_bir_lowering=False)

    x_fm = nc.dram_tensor("x_fm", [C, ROWS_PER_CORE], F32, kind="ExternalInput").ap()
    vecs = {}
    for name in ("tmk", "tmv", "tmr", "xxv", "tfv", "aav", "bbv", "ppv"):
        vecs[name] = nc.dram_tensor(name, [128, NCHUNK], F32, kind="ExternalInput").ap()
    bias_col = nc.dram_tensor("bias_col", [NEXP, 1], F32, kind="ExternalInput").ap()
    bias_b = nc.dram_tensor("bias_b", [128, NEXP], F32, kind="ExternalInput").ap()
    eye_d = nc.dram_tensor("eye", [128, 128], F32, kind="ExternalInput").ap()
    wkT = nc.dram_tensor("wkT", [8, 128, 4096], F32, kind="ExternalInput").ap()
    wvT = nc.dram_tensor("wvT", [8, 128, 4096], F32, kind="ExternalInput").ap()
    wrT = nc.dram_tensor("wrT", [8, 128, 4096], F32, kind="ExternalInput").ap()
    woT = nc.dram_tensor("woT", [8, 128, 4096], F32, kind="ExternalInput").ap()
    gwT = nc.dram_tensor("gwT", [128, NCHUNK * NEXP], F32, kind="ExternalInput").ap()
    gT = nc.dram_tensor("gT", [NEXP, 4, 128, 4096], F32, kind="ExternalInput").ap()
    uT = nc.dram_tensor("uT", [NEXP, 4, 128, 4096], F32, kind="ExternalInput").ap()
    dT = nc.dram_tensor("dT", [NEXP, 8, 128, 2048], F32, kind="ExternalInput").ap()
    sgT = nc.dram_tensor("sgT", [4, 128, 4096], F32, kind="ExternalInput").ap()
    suT = nc.dram_tensor("suT", [4, 128, 4096], F32, kind="ExternalInput").ap()
    sdT = nc.dram_tensor("sdT", [8, 128, 2048], F32, kind="ExternalInput").ap()
    dTm = nc.dram_tensor("dTm", [NEXP, 8, 128, 2048], F32, kind="ExternalInput").ap()
    zeros4m = nc.dram_tensor("zeros4m", [TPASS, C], F32, kind="ExternalInput").ap()
    fill512 = nc.dram_tensor("fill512", [NEXP * CAP, 1], mybir.dt.uint32, kind="ExternalInput").ap()
    y1tm = nc.dram_tensor("y1tm", [TPASS + 1, 2 * C], F16).ap()
    cwtmd = nc.dram_tensor("cwtmd", [TPASS + 1, NEXP], F32).ap()
    idxd = nc.dram_tensor("idxd", [NEXP * CAP, 1], mybir.dt.uint32).ap()
    routedtm = nc.dram_tensor("routedtm", [TPASS + NSC * 128, C], F32).ap()
    y_fm = nc.dram_tensor("y_fm", [C, ROWS_PER_CORE], F32, kind="ExternalOutput").ap()

    p = Prog()

    with ExitStack() as stack:
        e_ = stack.enter_context
        A32 = e_(nc.sbuf_tensor("A32", [128, NCHUNK * TPASS], F32))
        MH1 = e_(nc.sbuf_tensor("MH1", [128, NCHUNK * TPASS], F16))
        ML1 = e_(nc.sbuf_tensor("ML1", [128, NCHUNK * TPASS], F16))
        MH2 = e_(nc.sbuf_tensor("MH2", [128, NCHUNK * TPASS], F16))
        ML2 = e_(nc.sbuf_tensor("ML2", [128, NCHUNK * TPASS], F16))
        F32B = e_(nc.sbuf_tensor("F32B", [128, NCHUNK * TPASS], F32))
        WRAW = e_(nc.sbuf_tensor("WRAW", [128, WStream.N_RAW * 2048], F32))
        WHI = e_(nc.sbuf_tensor("WHI", [128, WStream.N_HL * 2048], F16))
        WLO = e_(nc.sbuf_tensor("WLO", [128, WStream.N_HL * 2048], F16))
        SCR = [e_(nc.sbuf_tensor(f"SCR{i}", [128, TPASS], F32)) for i in range(5)]
        CWSB = None if SPARSE else e_(nc.sbuf_tensor("CWSB", [128, TPASS], F32))
        SEM_ = e_(nc.sbuf_tensor("SEM_", [NEXP, 2 * TPASS], F32))
        TM = e_(nc.sbuf_tensor("TM", [128, 18 * NEXP + 16], F32))
        FIX = e_(nc.sbuf_tensor("FIX", [128, 10 * NCHUNK], F32))
        CONST = {}
        for name in ("tmk", "tmv", "tmr", "xxv", "tfv", "aav", "bbv", "ppv",
                     "c2k", "c2v", "c2r"):
            CONST[name] = e_(nc.sbuf_tensor("c_" + name, [128, NCHUNK], F32))
        BIASC = e_(nc.sbuf_tensor("BIASC", [NEXP, 1], F32))
        BIASB = e_(nc.sbuf_tensor("BIASB", [128, NEXP], F32))
        EYE = e_(nc.sbuf_tensor("EYE", [128, 128], F32))
        ONES = e_(nc.sbuf_tensor("ONES", [128, 1], F32))
        ONESR = e_(nc.sbuf_tensor("ONESR", [1, 128], F32))
        NEGT = e_(nc.sbuf_tensor("NEGT", [128, NEXP], F32))
        EPSC = e_(nc.sbuf_tensor("EPSC", [128, 1], F32))
        GFH = MH1[:, 0:NCHUNK * CAP]
        GFL = ML1[:, 0:NCHUNK * CAP]
        CWG = e_(nc.sbuf_tensor("CWG", [128, 2 * NSC * NEXP], F32))
        ONES16 = e_(nc.sbuf_tensor("ONES16", [NEXP, TPASS], F32))
        TOKIO = e_(nc.sbuf_tensor("TOKIO", [128, 4], F32))
        TIDU = e_(nc.sbuf_tensor("TIDU", [128, 4], mybir.dt.uint32))
        DUMPV = e_(nc.sbuf_tensor("DUMPV", [128, NSC], F32))
        IDXS = e_(nc.sbuf_tensor("IDXS", [128, 2 * NSC], mybir.dt.uint32))
        IDXF = e_(nc.sbuf_tensor("IDXF", [128, 2 * NSC], F32))
        SCATU = e_(nc.sbuf_tensor("SCATU", [128, 2 * NSC], mybir.dt.uint32))
        SCM = e_(nc.sbuf_tensor("SCM", [128, 2 * NSC], F32))
        SCN = e_(nc.sbuf_tensor("SCN", [128, 2 * NSC], F32))
        BTM = e_(nc.sbuf_tensor("BTM", [128, 4 * NEXP], mybir.dt.uint32))
        GWS = e_(nc.sbuf_tensor("GWS", [128, NCHUNK * NEXP], F32))
        GWHI = e_(nc.sbuf_tensor("GWHI", [128, NCHUNK * NEXP], F16))
        GWLO = e_(nc.sbuf_tensor("GWLO", [128, NCHUNK * NEXP], F16))
        PB = [e_(nc.psum_tensor(f"pb{i}", [128, 512], F32)) for i in range(8)]

        sems = {s: e_(nc.semaphore("sem_" + s)) for s in ENGINES}
        for s in Prog.DMA_SEMS:
            sems[s] = e_(nc.semaphore("sem_" + s))

        rb_cap = nc.gpsimd.alloc_register("rb_cap")
        rb_tp = nc.gpsimd.alloc_register("rb_tp")

        ws = WStream(p, WRAW, WHI, WLO)

        def chunk(t, c):
            return t[:, c * TPASS:(c + 1) * TPASS]

        def col(t, j):
            return t[:, j:j + 1]

        bank_evac = {b: [] for b in range(8)}

        def mm3(pe_after, bank, whi, wlo, mhi, mlo, start, stop, inc=0, width=TPASS):
            p.op("tensor",
                 lambda e, b=bank, w=whi, m=mhi, s=start: e.matmul(PB[b][:, 0:width], w, m, start=s, stop=False),
                 after=pe_after)
            p.op("tensor",
                 lambda e, b=bank, w=whi, m=mlo: e.matmul(PB[b][:, 0:width], w, m, start=False, stop=False))
            return p.op("tensor",
                        lambda e, b=bank, w=wlo, m=mhi, s=stop: e.matmul(PB[b][:, 0:width], w, m, start=False, stop=s),
                        inc=inc)

        def proj_og(w_dram_og, pair, mhi_t, mlo_t, mix_toks):
            pe_last = None
            for half in range(2):
                s, hi, lo, sp_toks = ws.stage(w_dram_og[:, half * 2048:(half + 1) * 2048])
                for cr in range(8):
                    c = half * 8 + cr
                    for oh in range(2):
                        b = pair[oh]
                        whi = hi[:, cr * 256 + oh * 128: cr * 256 + (oh + 1) * 128]
                        wlo = lo[:, cr * 256 + oh * 128: cr * 256 + (oh + 1) * 128]
                        start = (half == 0 and cr == 0)
                        stop = (half == 1 and cr == 7)
                        after = list(sp_toks)
                        if cr == 0:
                            after += [mix_toks[min(half * 8 + 7, len(mix_toks) - 1)]]
                            after += bank_evac[b]
                        pe_last = mm3(after, b, whi, wlo, chunk(mhi_t, c), chunk(mlo_t, c),
                                      start, stop, inc=(1 if (cr == 7 and oh == 1) else 0))
                ws.consumed(s, pe_last)
            return pe_last

        def down_og(d_dram_og, pair, h_last_tok):
            s, hi, lo, sp_toks = ws.stage(d_dram_og, width=2048)
            pe_last = None
            for ic in range(ICHUNK):
                for oh in range(2):
                    b = pair[oh]
                    whi = hi[:, ic * 256 + oh * 128: ic * 256 + (oh + 1) * 128]
                    wlo = lo[:, ic * 256 + oh * 128: ic * 256 + (oh + 1) * 128]
                    after = list(sp_toks)
                    if ic == 0:
                        after += [h_last_tok] + bank_evac[b]
                    pe_last = mm3(after, b, whi, wlo, chunk(MH2, ic), chunk(ML2, ic),
                                  ic == 0, ic == ICHUNK - 1,
                                  inc=(1 if (ic == ICHUNK - 1 and oh == 1) else 0))
            ws.consumed(s, pe_last)
            return pe_last

        for name in ("tmk", "tmv", "tmr", "xxv", "tfv", "aav", "bbv", "ppv"):
            p.dma_op(lambda e, o=CONST[name], i_=vecs[name]: e.dma_start(out=o[:, :], in_=i_))
        p.dma_op(lambda e: e.dma_start(out=BIASC[:, :], in_=bias_col))
        p.dma_op(lambda e: e.dma_start(out=BIASB[:, :], in_=bias_b))
        p.dma_op(lambda e: e.dma_start(out=EYE[:, :], in_=eye_d))
        p.dma_op(lambda e: e.dma_start(out=GWS[:, :], in_=gwT))
        p.op("gpsimd", lambda e: e.memset(ONES[:, :], 1.0), inc=1)
        p.op("gpsimd", lambda e: e.memset(ONESR[:, :], 1.0), inc=1)
        p.op("gpsimd", lambda e: e.memset(NEGT[:, :], -1e30), inc=1)
        p.op("gpsimd", lambda e: e.memset(EPSC[:, :], EPS), inc=1)
        p.op("gpsimd", lambda e: e.reg_mov(rb_cap, CAP - 1), inc=1)
        p.op("gpsimd", lambda e: e.reg_mov(rb_tp, TPASS), inc=1)
        p.op("gpsimd", lambda e: e.memset(ONES16[:, :], 1.0), inc=1)
        p.op("gpsimd", lambda e: e.iota(TOKIO[:, :], [[128, 4]], channel_multiplier=1,
                                        allow_small_or_imprecise_dtypes=True), inc=1)
        for sc in range(NSC):
            p.op("gpsimd", lambda e, sc=sc: e.iota(DUMPV[:, sc:sc + 1], [[1, 1]],
                                                   base=TPASS + sc * 128, channel_multiplier=1,
                                                   allow_small_or_imprecise_dtypes=True), inc=1)
        p.barrier()
        p.op("vector", lambda e: e.tensor_copy(TIDU[:, :], TOKIO[:, :]), inc=1)
        p.dma_op(lambda e: e.dma_start(out=y1tm[TPASS:TPASS + 1, :], in_=zeros4m[0:2, :]),
                 sem="gd", eng="gpsimd")
        p.dma_op(lambda e: e.dma_start(out=cwtmd[TPASS:TPASS + 1, :], in_=zeros4m[0:1, 0:NEXP]), sem="dma")
        p.barrier()
        for nm, src in (("c2k", "tmk"), ("c2v", "tmv"), ("c2r", "tmr")):
            p.op("vector", lambda e, o=CONST[nm], a=CONST[src]:
                 e.tensor_scalar(o[:, :], a[:, :], -1.0, 1.0, AluOpType.mult, AluOpType.add), inc=1)
            p.op("vector", lambda e, o=CONST[nm]:
                 e.tensor_tensor(o[:, :], o[:, :], CONST["xxv"][:, :], AluOpType.mult), inc=1)
        tgw = p.op("scalar", lambda e: e.copy(GWHI[:, :], GWS[:, :]), inc=1)
        p.op("vector", lambda e: e.tensor_tensor(GWLO[:, :], GWS[:, :], GWHI[:, :], AluOpType.subtract),
             after=[tgw], inc=1)
        p.barrier()

        carry = {"pe": None, "stores": None, "a32_free": None}
        for h in range(n_pass):
            tok0 = h * TPASS

            x_tok = []
            for c in range(NCHUNK):
                x_tok.append(p.dma_op(
                    lambda e, c=c, t0=tok0: e.dma_start(
                        out=chunk(A32, c),
                        in_=x_fm[c * 128:(c + 1) * 128, t0:t0 + TPASS]),
                    after=[carry["a32_free"]] if c == 0 else ()))

            def build_mix(hi_t, lo_t, tm_name, c2_name, extra=()):
                toks = []
                prev_hi = None
                for c in range(NCHUNK):
                    t0 = p.op("vector", lambda e, c=c, tm=tm_name, c2=c2_name:
                              e.tensor_scalar(SCR[4][:, :], chunk(A32, c),
                                              col(CONST[tm], c), col(CONST[c2], c),
                                              AluOpType.mult, AluOpType.add),
                              after=[x_tok[-1], prev_hi] + list(extra), inc=1)
                    t1 = p.op("scalar", lambda e, c=c, ht=hi_t:
                              e.copy(chunk(ht, c), SCR[4][:, :]), after=[t0], inc=1)
                    t2 = p.op("vector", lambda e, c=c, ht=hi_t, lt=lo_t:
                              e.tensor_tensor(chunk(lt, c), SCR[4][:, :], chunk(ht, c),
                                              AluOpType.subtract), after=[t1], inc=1)
                    prev_hi = t1
                    toks.append(t2)
                return toks

            jobs = []
            st = {}

            def add_job(units, mm_fn, ew_fn=None):
                jobs.append((units, mm_fn, ew_fn))

            def og_units(w_og):
                return [(w_og[:, 0:2048], 2048), (w_og[:, 2048:4096], 2048)]

            def mms_og(staged, pair, mhi_t, mlo_t, mix_toks, width=TPASS):
                pe_last = None
                for half in range(2):
                    s, hi, lo, sp_toks = staged[half]
                    for cr in range(8):
                        c = half * 8 + cr
                        for oh in range(2):
                            b = pair[oh]
                            whi = hi[:, cr * 256 + oh * 128: cr * 256 + (oh + 1) * 128]
                            wlo = lo[:, cr * 256 + oh * 128: cr * 256 + (oh + 1) * 128]
                            start = (half == 0 and cr == 0)
                            stop = (half == 1 and cr == 7)
                            after = list(sp_toks)
                            if cr == 0:
                                after += [mix_toks[min(half * 8 + 7, len(mix_toks) - 1)]]
                                after += bank_evac[b]
                            mh = mhi_t[:, c * width:(c + 1) * width]
                            ml = mlo_t[:, c * width:(c + 1) * width]
                            pe_last = mm3(after, b, whi, wlo, mh, ml,
                                          start, stop, inc=(1 if (cr == 7 and oh == 1) else 0),
                                          width=width)
                    ws.consumed(s, pe_last)
                return pe_last

            def mms_down(staged, pair, h_toks):
                s, hi, lo, sp_toks = staged[0]
                pe_last = None
                for ic in range(ICHUNK):
                    for oh in range(2):
                        b = pair[oh]
                        whi = hi[:, ic * 256 + oh * 128: ic * 256 + (oh + 1) * 128]
                        wlo = lo[:, ic * 256 + oh * 128: ic * 256 + (oh + 1) * 128]
                        after = list(sp_toks)
                        if ic == 0:
                            after += [h_toks[-1]] + bank_evac[b]
                        pe_last = mm3(after, b, whi, wlo, chunk(MH2, ic), chunk(ML2, ic),
                                      ic == 0, ic == ICHUNK - 1,
                                      inc=(1 if (ic == ICHUNK - 1 and oh == 1) else 0))
                ws.consumed(s, pe_last)
                return pe_last

            def kv_ew(og, pairK, pairV):
                for oh in range(2):
                    oc = og * 2 + oh
                    bK, bV = pairK[oh], pairV[oh]
                    t0 = p.op("vector", lambda e, bK=bK, oc=oc:
                              e.tensor_scalar(SCR[0][:, :], PB[bK][:, :], col(CONST["tfv"], oc),
                                              0.0, AluOpType.add, AluOpType.min),
                              after=[st[("tK", og)], st[("tV", og)], st.get("prev_exp")], inc=1)
                    ktoks = [t0]
                    vc_tok = None
                    if h == 0:
                        ktoks.append(p.op("vector", lambda e, bK=bK, oc=oc:
                                          e.tensor_copy(col(FIX, 0 * NCHUNK + oc), PB[bK][:, 0:1]),
                                          inc=1))
                        vc_tok = p.op("vector", lambda e, bV=bV, oc=oc:
                                      e.tensor_copy(col(FIX, 1 * NCHUNK + oc), PB[bV][:, 0:1]),
                                      inc=1)
                    t1 = p.op("scalar", lambda e: e.activation(SCR[1][:, :], SCR[0][:, :], AF.Exp),
                              after=[t0], inc=1)
                    t2 = p.op("scalar", lambda e: e.activation(SCR[2][:, :], SCR[0][:, :], AF.Exp, scale=-1.0),
                              after=[t0], inc=1)
                    st["prev_exp"] = t2
                    t3 = p.op("vector", lambda e, bV=bV:
                              e.tensor_tensor(SCR[3][:, :], SCR[1][:, :], PB[bV][:, :], AluOpType.mult),
                              after=[t1], inc=1)
                    t4 = p.op("vector", lambda e, oc=oc:
                              e.tensor_tensor(chunk(F32B, oc), SCR[3][:, :], SCR[2][:, :], AluOpType.mult),
                              after=[t2, t3, carry["stores"]], inc=1)
                    st["kv_last"] = t4
                    bank_evac[bK] = ktoks
                    bank_evac[bV] = [t3] + ([vc_tok] if vc_tok else [])

            def r_ew(og, pair):
                for oh in range(2):
                    oc = og * 2 + oh
                    b = pair[oh]
                    t0 = p.op("scalar", lambda e, b=b: e.activation(SCR[0][:, :], PB[b][:, :], AF.Sigmoid),
                              after=[st[("tR", og)], st["kv_last"], st.get("prev_sig")], inc=1)
                    rtoks = [t0]
                    if h == 0:
                        rtoks.append(p.op("scalar", lambda e, b=b, oc=oc:
                                          e.copy(col(FIX, 2 * NCHUNK + oc), PB[b][:, 0:1]),
                                          after=[st[("tR", og)]], inc=1))
                    t1 = p.op("vector", lambda e, oc=oc:
                              e.tensor_tensor(SCR[1][:, :], SCR[0][:, :], chunk(F32B, oc), AluOpType.mult),
                              after=[t0], inc=1)
                    t2 = p.op("scalar", lambda e, oc=oc: e.copy(chunk(MH2, oc), SCR[1][:, :]),
                              after=[t1], inc=1)
                    st["prev_sig"] = t2
                    t3 = p.op("vector", lambda e, oc=oc:
                              e.tensor_tensor(chunk(ML2, oc), SCR[1][:, :], chunk(MH2, oc), AluOpType.subtract),
                              after=[t2], inc=1)
                    st["rwkv_last"] = t3
                    bank_evac[b] = rtoks

            def do_fixup():
                if h != 0:
                    st["fix"] = st["rwkv_last"]
                    return
                fx = [FIX[:, i * NCHUNK:(i + 1) * NCHUNK] for i in range(10)]
                k0, v0, r0 = fx[0], fx[1], fx[2]
                ww0, qq0, d_, e1_, e2_, num_, den_ = fx[3], fx[4], fx[5], fx[6], fx[7], fx[8], fx[9]
                t = p.op("vector", lambda e: e.tensor_tensor(ww0, k0, CONST["tfv"][:, :], AluOpType.add),
                         after=[st["rwkv_last"]], inc=1)
                t = p.op("vector", lambda e: e.tensor_tensor(qq0, CONST["ppv"][:, :], ww0, AluOpType.max), after=[t], inc=1)
                t = p.op("vector", lambda e: e.tensor_tensor(d_, CONST["ppv"][:, :], qq0, AluOpType.subtract), after=[t], inc=1)
                t = p.op("scalar", lambda e: e.activation(e1_, d_, AF.Exp), after=[t], inc=1)
                t = p.op("vector", lambda e: e.tensor_tensor(d_, ww0, qq0, AluOpType.subtract), after=[t], inc=1)
                t = p.op("scalar", lambda e: e.activation(e2_, d_, AF.Exp), after=[t], inc=1)
                t = p.op("vector", lambda e: e.tensor_tensor(num_, e1_, CONST["aav"][:, :], AluOpType.mult), after=[t], inc=1)
                t = p.op("vector", lambda e: e.tensor_tensor(d_, e2_, v0, AluOpType.mult), after=[t], inc=1)
                t = p.op("vector", lambda e: e.tensor_tensor(num_, num_, d_, AluOpType.add), after=[t], inc=1)
                t = p.op("vector", lambda e: e.tensor_tensor(den_, e1_, CONST["bbv"][:, :], AluOpType.mult), after=[t], inc=1)
                t = p.op("vector", lambda e: e.tensor_tensor(den_, den_, e2_, AluOpType.add), after=[t], inc=1)
                t = p.op("vector", lambda e: e.reciprocal(den_, den_), after=[t], inc=1)
                t = p.op("vector", lambda e: e.tensor_tensor(num_, num_, den_, AluOpType.mult), after=[t], inc=1)
                t = p.op("scalar", lambda e: e.activation(d_, r0, AF.Sigmoid), after=[t], inc=1)
                t = p.op("vector", lambda e: e.tensor_tensor(num_, num_, d_, AluOpType.mult), after=[t], inc=1)
                rh3 = MH2[:, :].rearrange("p (c t) -> p c t", t=TPASS)[:, :, 0:1]
                rl3 = ML2[:, :].rearrange("p (c t) -> p c t", t=TPASS)[:, :, 0:1]
                nm3 = num_.rearrange("p (c o) -> p c o", o=1)
                t = p.op("scalar", lambda e: e.copy(rh3, nm3), after=[t], inc=1)
                t = p.op("vector", lambda e: e.tensor_tensor(rl3, nm3, rh3, AluOpType.subtract), after=[t], inc=1)
                st["fix"] = t

            def layernorm(src_t, hi_t, lo_t, in_toks, split):
                mm_last = None
                prev_sq_mm = None
                for c in range(NCHUNK):
                    tq = p.op("scalar", lambda e, c=c, st_=src_t: e.activation(SCR[0][:, :], chunk(st_, c), AF.Square),
                              after=[in_toks[c], prev_sq_mm], inc=1)
                    a6 = bank_evac[6] if c == 0 else []
                    a7 = bank_evac[7] if c == 0 else []
                    p.op("tensor", lambda e, c=c, st_=src_t: e.matmul(PB[6][0:1, :], ONES[:, :], chunk(st_, c),
                                                                     start=(c == 0), stop=(c == NCHUNK - 1)),
                         after=[in_toks[c]] + a6)
                    mm_last = p.op("tensor", lambda e, c=c: e.matmul(PB[7][0:1, :], ONES[:, :], SCR[0][:, :],
                                                                     start=(c == 0), stop=(c == NCHUNK - 1)),
                                   after=[tq] + a7, inc=1)
                    prev_sq_mm = mm_last
                mean = SCR[2][0:1, :]
                var = SCR[3][0:1, :]
                rstd = SCR[3][0:1, :]
                t = p.op("vector", lambda e: e.tensor_scalar(mean, PB[6][0:1, :], 1.0 / C, None,
                                                             AluOpType.mult, AluOpType.bypass),
                         after=[mm_last], inc=1)
                t = p.op("vector", lambda e: e.tensor_scalar(var, PB[7][0:1, :], 1.0 / C, None,
                                                             AluOpType.mult, AluOpType.bypass),
                         after=[t], inc=1)
                bank_evac[6] = [t]
                bank_evac[7] = [t]
                t = p.op("vector", lambda e: e.tensor_tensor(SCR[0][0:1, :], mean, mean, AluOpType.mult),
                         after=[t], inc=1)
                t = p.op("vector", lambda e: e.tensor_tensor(var, var, SCR[0][0:1, :], AluOpType.subtract),
                         after=[t], inc=1)
                t = p.op("scalar", lambda e: e.activation(SCR[0][0:1, :], var, AF.Sqrt, bias=EPSC[0:1, :]), after=[t], inc=1)
                t = p.op("vector", lambda e: e.reciprocal(rstd, SCR[0][0:1, :]), after=[t], inc=1)
                tb1 = p.op("tensor", lambda e: e.matmul(PB[4][:, :], ONESR[:, :], mean, start=True, stop=True),
                           after=[t] + bank_evac[4], inc=1)
                tb2 = p.op("tensor", lambda e: e.matmul(PB[5][:, :], ONESR[:, :], rstd, start=True, stop=True),
                           after=[t] + bank_evac[5], inc=1)
                out_toks = []
                for c in range(NCHUNK):
                    t0 = p.op("vector", lambda e, c=c, st_=src_t: e.tensor_tensor(SCR[1][:, :], chunk(st_, c),
                                                                                  PB[4][:, :], AluOpType.subtract),
                              after=[tb1, tb2], inc=1)
                    t1 = p.op("vector", lambda e, c=c, st_=src_t: e.tensor_tensor(chunk(st_, c), SCR[1][:, :],
                                                                                  PB[5][:, :], AluOpType.mult),
                              after=[t0], inc=1)
                    if split:
                        t2 = p.op("scalar", lambda e, c=c, st_=src_t, ht=hi_t: e.copy(chunk(ht, c), chunk(st_, c)),
                                  after=[t1], inc=1)
                        t3 = p.op("vector", lambda e, c=c, st_=src_t, ht=hi_t, lt=lo_t:
                                  e.tensor_tensor(chunk(lt, c), chunk(st_, c), chunk(ht, c), AluOpType.subtract),
                                  after=[t2], inc=1)
                        out_toks.append((t1, t3))
                    else:
                        out_toks.append((t1, t1))
                bank_evac[4] = [out_toks[-1][0]]
                bank_evac[5] = [out_toks[-1][0]]
                return out_toks

            def ln1_routing():
                ln1 = layernorm(A32, MH1, ML1, st["res_toks"], split=True)
                st["y1n"] = [a for (a, b) in ln1]
                st["y1"] = [b for (a, b) in ln1]
                if attn_only:
                    for c in range(NCHUNK):
                        p.dma_op(lambda e, c=c, t0=tok0: e.dma_start(
                            out=y_fm[c * 128:(c + 1) * 128, t0:t0 + TPASS],
                            in_=chunk(A32, c)),
                            after=[st["y1"][c]], sem="out")
                    return
                lg_last = None
                for c in range(NCHUNK):
                    ghi = GWHI[:, c * NEXP:(c + 1) * NEXP]
                    glo = GWLO[:, c * NEXP:(c + 1) * NEXP]
                    a6 = bank_evac[6] if c == 0 else []
                    p.op("tensor", lambda e, g=ghi, c=c: e.matmul(PB[6][0:NEXP, :], g, chunk(MH1, c),
                                                                  start=(c == 0), stop=False),
                         after=[st["y1"][c]] + a6)
                    p.op("tensor", lambda e, g=ghi, c=c: e.matmul(PB[6][0:NEXP, :], g, chunk(ML1, c),
                                                                  start=False, stop=False))
                    lg_last = p.op("tensor", lambda e, g=glo, c=c: e.matmul(PB[6][0:NEXP, :], g, chunk(MH1, c),
                                                                            start=False, stop=(c == NCHUNK - 1)),
                                   inc=(1 if c == NCHUNK - 1 else 0))
                sc_em = SEM_[:, 0:TPASS]
                bi_em = SEM_[:, TPASS:2 * TPASS]
                tsg = p.op("scalar", lambda e: e.activation(sc_em, PB[6][0:NEXP, :], AF.Sigmoid),
                           after=[lg_last], inc=1)
                bank_evac[6] = [tsg]
                tbi = p.op("vector", lambda e: e.tensor_scalar(bi_em, sc_em, BIASC[:, :], None,
                                                               AluOpType.add, AluOpType.bypass),
                           after=[tsg], inc=1)
                tr_last = None
                for tc in range(4):
                    tr_last = p.op("tensor", lambda e, tc=tc: e.transpose(PB[7][:, tc * NEXP:(tc + 1) * NEXP],
                                                                          bi_em[:, tc * 128:(tc + 1) * 128],
                                                                          EYE[0:NEXP, 0:NEXP]),
                                   after=[tbi] + (bank_evac[7] if tc == 0 else []), inc=(1 if tc == 3 else 0))
                tm_tok = p.op("vector", lambda e: e.tensor_copy(TM[:, 0:4 * NEXP], PB[7][:, 0:4 * NEXP]),
                              after=[tr_last], inc=1)
                bank_evac[7] = [tm_tok]
                cw_toks = []
                for tc in range(4):
                    bia = TM[:, tc * NEXP:(tc + 1) * NEXP]
                    sel = TM[:, (4 + tc) * NEXP:(5 + tc) * NEXP]
                    wk_ = TM[:, (8 + tc) * NEXP:(9 + tc) * NEXP]
                    msk = TM[:, (12 + tc) * NEXP:(13 + tc) * NEXP]
                    mx = TM[:, 16 * NEXP + tc: 16 * NEXP + tc + 1]
                    t = p.op("vector", lambda e, sel=sel: e.memset(sel, 0.0), after=[tm_tok], inc=1)
                    t = p.op("vector", lambda e, wk_=wk_, bia=bia: e.tensor_copy(wk_, bia), after=[t], inc=1)
                    for it in range(4):
                        t = p.op("vector", lambda e, mx=mx, wk_=wk_: e.reduce_max(mx, wk_, axis=AX), after=[t], inc=1)
                        t = p.op("vector", lambda e, msk=msk, wk_=wk_, mx=mx:
                                 e.tensor_scalar(msk, wk_, mx, None, AluOpType.is_ge, AluOpType.bypass),
                                 after=[t], inc=1)
                        t = p.op("vector", lambda e, sel=sel, msk=msk:
                                 e.tensor_tensor(sel, sel, msk, AluOpType.max), after=[t], inc=1)
                        if it < 3:
                            t = p.op("vector", lambda e, wk_=wk_, msk=msk:
                                     e.scalar_tensor_tensor(wk_, msk, -1e30, wk_,
                                                            AluOpType.mult, AluOpType.add),
                                     after=[t], inc=1)
                    t = p.op("vector", lambda e, msk=msk, bia=bia:
                             e.tensor_tensor(msk, bia, BIASB[:, :], AluOpType.subtract), after=[t], inc=1)
                    t = p.op("vector", lambda e, msk=msk, sel=sel:
                             e.tensor_tensor(msk, msk, sel, AluOpType.mult), after=[t], inc=1)
                    t = p.op("vector", lambda e, mx=mx, msk=msk: e.reduce_sum(mx, msk, axis=AX), after=[t], inc=1)
                    t = p.op("vector", lambda e, mx=mx: e.reciprocal(mx, mx), after=[t], inc=1)
                    t = p.op("vector", lambda e, msk=msk, mx=mx:
                             e.tensor_scalar(msk, msk, mx, None, AluOpType.mult, AluOpType.bypass),
                             after=[t], inc=1)
                    cw_toks.append(t)
                st["cw_toks"] = cw_toks

                if SPARSE:
                    fence = p.dma_totals() + [st.get("y1tm_done")]
                    tr = None
                    for part, (srcT, dstoff) in enumerate(((MH1, 0), (ML1, C))):
                        for j in range(NCHUNK):
                            stage3 = (MH2 if part == 0 else ML2)[:, :].rearrange(
                                "p (a b c) -> p a b c", b=NCHUNK, c=128)[:, :, j, :]
                            tr = p.dma_op(
                                lambda e, j=j, s3=stage3, srcT=srcT:
                                e.dma_start_transpose(s3, chunk(srcT, j)),
                                after=(fence if (part == 0 and j == 0) else []) +
                                      [st["y1"][j]],
                                sem="dma", fence=True)
                        p.xfence = tr
                        for th in range(4):
                            stg = (MH2 if part == 0 else ML2)[:, :].rearrange(
                                "p (a b) -> p a b", b=2048)[:, th, :]
                            tr2 = p.dma_op(
                                lambda e, th=th, stg=stg, dstoff=dstoff:
                                e.dma_start(out=y1tm[th * 128:(th + 1) * 128,
                                                     dstoff:dstoff + C], in_=stg),
                                sem="dma", fence=(part == 1 and th == 3))
                    st["y1tm_done"] = tr2
                    cwst = None
                    for tc in range(4):
                        msk = TM[:, (12 + tc) * NEXP:(13 + tc) * NEXP]
                        cwst = p.dma_op(
                            lambda e, tc=tc, msk=msk:
                            e.dma_start(out=cwtmd[tc * 128:(tc + 1) * 128, :], in_=msk),
                            after=[cw_toks[tc], ("gd", p.dmac["gd"])], sem="dma",
                            fence=(tc == 3))
                    st["cw_st"] = cwst
                    tsel = None
                    for tc in range(4):
                        sel = TM[:, (4 + tc) * NEXP:(5 + tc) * NEXP]
                        tsel = p.op("tensor", lambda e, tc=tc, sel=sel:
                                    e.transpose(PB[6][0:NEXP, tc * 128:(tc + 1) * 128], sel,
                                                EYE[:, :]),
                                    after=[cw_toks[tc]] + (bank_evac[6] if tc == 0 else []),
                                    inc=(1 if tc == 3 else 0))
                    sel_em = SEM_[:, 0:TPASS]
                    pos_em = SEM_[:, TPASS:2 * TPASS]
                    t = p.op("vector", lambda e: e.tensor_copy(sel_em, PB[6][0:NEXP, 0:TPASS]),
                             after=[tsel], inc=1)
                    bank_evac[6] = [t]
                    t = p.op("vector", lambda e: e.tensor_tensor_scan(pos_em, ONES16[:, :], sel_em, 0.0,
                                                                      AluOpType.mult, AluOpType.add),
                             after=[t], inc=1)
                    t = p.op("vector", lambda e: e.tensor_scalar(pos_em, pos_em, -1000.0, None,
                                                                 AluOpType.add, AluOpType.bypass),
                             after=[t], inc=1)
                    t = p.op("vector", lambda e: e.tensor_tensor(pos_em, pos_em, sel_em, AluOpType.mult),
                             after=[t], inc=1)
                    t = p.op("vector", lambda e: e.tensor_scalar(pos_em, pos_em, 999.0, None,
                                                                 AluOpType.add, AluOpType.bypass),
                             after=[t], inc=1)
                    bpos_tok = t
                    tb_ = None
                    for tc in range(4):
                        tb_ = p.op("tensor", lambda e, tc=tc:
                                   e.transpose(PB[7][:, tc * NEXP:(tc + 1) * NEXP],
                                               pos_em[:, tc * 128:(tc + 1) * 128],
                                               EYE[0:NEXP, 0:NEXP]),
                                   after=[bpos_tok] + (bank_evac[7] if tc == 0 else []),
                                   inc=(1 if tc == 3 else 0))
                    t = p.op("vector", lambda e: e.tensor_copy(TM[:, 8 * NEXP:12 * NEXP],
                                                               PB[7][:, 0:4 * NEXP]),
                             after=[tb_], inc=1)
                    bank_evac[7] = [t]
                    btm_tok = p.op("vector", lambda e: e.tensor_copy(BTM[:, :], TM[:, 8 * NEXP:12 * NEXP]),
                                   after=[t], inc=1)
                    pf = p.dma_op(lambda e: e.dma_start(out=idxd, in_=fill512),
                                  after=p.dma_totals(), sem="dma", fence=True)
                    sc_last = None
                    for ex2 in range(NEXP):
                        for tc in range(4):
                            sc_last = p.dma_op(
                                lambda e, ex2=ex2, tc=tc: e.indirect_dma_start(
                                    out=idxd,
                                    out_offset=bass.IndirectOffsetOnAxis(
                                        BTM[:, tc * NEXP + ex2: tc * NEXP + ex2 + 1], 0),
                                    in_=TIDU[:, tc:tc + 1], in_offset=None,
                                    element_offset=ex2 * CAP,
                                    bounds_check=rb_cap, oob_is_err=False),
                                after=[pf, btm_tok], sem="gd", eng="gpsimd")
                    st["idx_ready"] = sc_last

            def prep_sparse(ex):
                par = ex % 2
                po = par * NSC
                idxv = idxd[ex * CAP:(ex + 1) * CAP, :].rearrange("(p sc) one -> p (sc one)", p=128)
                tl = p.dma_op(lambda e, iv=idxv, po=po: e.dma_start(out=IDXS[:, po:po + NSC], in_=iv),
                              after=[st["idx_ready"], st.get("scat_prev")] + st.get(("idx_free", par), []),
                              sem="dma", fence=True)
                t = p.op("vector", lambda e, po=po: e.tensor_copy(IDXF[:, po:po + NSC], IDXS[:, po:po + NSC]),
                         after=[tl], inc=1)
                t = p.op("vector", lambda e, po=po: e.tensor_scalar(SCM[:, po:po + NSC], IDXF[:, po:po + NSC],
                                                                    float(TPASS), None,
                                                                    AluOpType.is_ge, AluOpType.bypass),
                         after=[t], inc=1)
                for sc in range(NSC):
                    j = po + sc
                    t = p.op("vector", lambda e, j=j, sc=sc: e.tensor_tensor(SCN[:, j:j + 1], DUMPV[:, sc:sc + 1],
                                                                             IDXF[:, j:j + 1], AluOpType.subtract),
                             after=[t], inc=1)
                    t = p.op("vector", lambda e, j=j: e.tensor_tensor(SCN[:, j:j + 1], SCN[:, j:j + 1],
                                                                      SCM[:, j:j + 1], AluOpType.mult),
                             after=[t], inc=1)
                    t = p.op("vector", lambda e, j=j: e.tensor_tensor(SCN[:, j:j + 1], SCN[:, j:j + 1],
                                                                      IDXF[:, j:j + 1], AluOpType.add),
                             after=[t], inc=1)
                scat_tok = p.op("vector", lambda e, po=po: e.tensor_copy(SCATU[:, po:po + NSC], SCN[:, po:po + NSC]),
                                after=[t], inc=1)
                st[("scatu_tok", par)] = scat_tok
                g_last = None
                for sc in range(NSC):
                    stg = (MH2 if sc == 0 else ML2)[:, 4096:8192]
                    g_last = p.dma_op(
                        lambda e, sc=sc, stg=stg, po=po: e.indirect_dma_start(
                            out=stg, out_offset=None, in_=y1tm,
                            in_offset=bass.IndirectOffsetOnAxis(IDXS[:, po + sc:po + sc + 1], 0),
                            bounds_check=rb_tp, oob_is_err=False),
                        after=[tl, st["y1tm_done"]] + st.get(("gstage_free", sc), []),
                        sem="gd", eng="gpsimd")
                    g_last = p.dma_op(
                        lambda e, sc=sc, po=po: e.indirect_dma_start(
                            out=CWG[:, (po + sc) * NEXP:(po + sc + 1) * NEXP], out_offset=None, in_=cwtmd,
                            in_offset=bass.IndirectOffsetOnAxis(IDXS[:, po + sc:po + sc + 1], 0)),
                        after=[st["cw_st"]], sem="gd", eng="gpsimd")
                st[("cw_gth", par)] = g_last
                fence = p.dma_totals()
                tr = None
                for sc in range(NSC):
                    stg_t = MH2 if sc == 0 else ML2
                    for part, dstT in enumerate((GFH, GFL)):
                        src_ap = stg_t[:, 4096 + part * 2048: 4096 + (part + 1) * 2048]
                        dst3 = dstT.rearrange("p (a b) -> p a b", b=CAP)[:, :, sc * 128:(sc + 1) * 128]
                        tr = p.dma_op(
                            lambda e, s_=src_ap, d_=dst3: e.dma_start_transpose(d_, s_),
                            after=(fence if (sc == 0 and part == 0) else []) + st.get("gf_free", []),
                            sem="dma", fence=True)
                p.xfence = tr
                st["gth_tok"] = tr
                st[("gstage_free", 0)] = [tr]
                st[("gstage_free", 1)] = [tr]
                st[("idx_free", par)] = [tr, st.get("scat_prev")]

            def prep_cw(ex):
                cw_toks = st["cw_toks"]
                ttr = None
                for tc in range(4):
                    cwcol = TM[:, (12 + tc) * NEXP + ex:(12 + tc) * NEXP + ex + 1]
                    ttr = p.op("tensor", lambda e, tc=tc, cw=cwcol:
                               e.transpose(PB[6][0:1, tc * 128:(tc + 1) * 128], cw, EYE[:, :]),
                               after=[cw_toks[tc]] + (bank_evac[6] if tc == 0 else []),
                               inc=(1 if tc == 3 else 0))
                tev = p.op("vector", lambda e: e.tensor_copy(SCR[0][0:1, :], PB[6][0:1, 0:TPASS]),
                           after=[ttr], inc=1)
                bank_evac[6] = [tev]
                tb = p.op("tensor", lambda e: e.matmul(PB[7][:, :], ONESR[:, :], SCR[0][0:1, :],
                                                       start=True, stop=True),
                          after=[tev] + bank_evac[7], inc=1)
                tc_ = p.op("vector", lambda e: e.tensor_copy(CWSB[:, :], PB[7][:, :]),
                           after=[tb], inc=1)
                bank_evac[7] = [tc_]
                return tc_

            def silu_ew(og, pairG, pairU, cw_tok):
                for oh in range(2):
                    ic = og * 2 + oh
                    bG, bU = pairG[oh], pairU[oh]
                    t0 = p.op("scalar", lambda e, bG=bG: e.activation(SCR[0][:, :], PB[bG][:, :], AF.Silu),
                              after=[st[("tG", og)], st[("tU", og)], st.get("prev_silu"),
                                     st.get("y1tm_done")], inc=1)
                    t1 = p.op("vector", lambda e, bU=bU:
                              e.tensor_tensor(SCR[1][:, :], SCR[0][:, :], PB[bU][:, :], AluOpType.mult),
                              after=[t0], inc=1)
                    if cw_tok is not None:
                        t1 = p.op("vector", lambda e:
                                  e.tensor_tensor(SCR[1][:, :], SCR[1][:, :], CWSB[:, :], AluOpType.mult),
                                  after=[cw_tok], inc=1)
                    t2 = p.op("scalar", lambda e, ic=ic: e.copy(chunk(MH2, ic), SCR[1][:, :]),
                              after=[t1], inc=1)
                    st["prev_silu"] = t2
                    t3 = p.op("vector", lambda e, ic=ic:
                              e.tensor_tensor(chunk(ML2, ic), SCR[1][:, :], chunk(MH2, ic), AluOpType.subtract),
                              after=[t2], inc=1)
                    st["h_toks"][ic] = t3
                    bank_evac[bG] = [t0]
                    bank_evac[bU] = [t1]

            def routed_evac(og, pair, first):
                for oh in range(2):
                    oc = og * 2 + oh
                    b = pair[oh]
                    if first:
                        t0 = p.op("vector", lambda e, b=b, oc=oc:
                                  e.tensor_tensor(chunk(F32B, oc), PB[b][:, :], chunk(A32, oc), AluOpType.add),
                                  after=[st[("tD", og)], st["y1n"][oc]], inc=1)
                    else:
                        t0 = p.op("vector", lambda e, b=b, oc=oc:
                                  e.tensor_tensor(chunk(F32B, oc), PB[b][:, :], chunk(F32B, oc), AluOpType.add),
                                  after=[st[("tD", og)]], inc=1)
                    bank_evac[b] = [t0]
                    st["routed_last"] = t0

            def ln2_stores():
                ln2 = layernorm(F32B, None, None, [st["routed_last"]] * NCHUNK, split=False)
                y2_toks = [a for (a, b) in ln2]
                for c in range(NCHUNK):
                    p.dma_op(lambda e, c=c, t0=tok0: e.dma_start(
                        out=y_fm[c * 128:(c + 1) * 128, t0:t0 + TPASS],
                        in_=chunk(F32B, c)),
                        after=[y2_toks[c]], sem="out")

            def silu_ew_sp(og):
                for oh in range(2):
                    ic = og * 2 + oh
                    bG, bU = ((0, 1) if og % 2 == 0 else (2, 3))[oh], ((4, 5) if og % 2 == 0 else (6, 7))[oh]
                    t0 = p.op("scalar", lambda e, bG=bG: e.activation(SCR[0][:, 0:CAP], PB[bG][:, 0:CAP], AF.Silu),
                              after=[st[("tG", og)], st[("tU", og)], st.get("prev_silu")], inc=1)
                    t1 = p.op("vector", lambda e, bU=bU:
                              e.tensor_tensor(SCR[1][:, 0:CAP], SCR[0][:, 0:CAP], PB[bU][:, 0:CAP], AluOpType.mult),
                              after=[t0], inc=1)
                    t2 = p.op("scalar", lambda e, ic=ic: e.copy(MH2[:, ic * CAP:(ic + 1) * CAP], SCR[1][:, 0:CAP]),
                              after=[t1], inc=1)
                    st["prev_silu"] = t2
                    t3 = p.op("vector", lambda e, ic=ic:
                              e.tensor_tensor(ML2[:, ic * CAP:(ic + 1) * CAP], SCR[1][:, 0:CAP],
                                              MH2[:, ic * CAP:(ic + 1) * CAP], AluOpType.subtract),
                              after=[t2], inc=1)
                    st["h_toks"][ic] = t3
                    bank_evac[bG] = [t0]
                    bank_evac[bU] = [t1]

            def mm_down_sp(sg, ic):
                s, hi, lo, sp_toks = sg[0]
                pe_last = None
                for sc in range(NSC):
                    h_hi = MH2[:, ic * CAP + sc * 128: ic * CAP + (sc + 1) * 128]
                    h_lo = ML2[:, ic * CAP + sc * 128: ic * CAP + (sc + 1) * 128]
                    for ot in range(4):
                        b = sc * 4 + ot
                        after = list(sp_toks)
                        if ot == 0 and sc == 0:
                            after += [st["h_toks"][ic]]
                        if ic == 0:
                            after += bank_evac[b]
                        pe_last = mm3(after, b, h_hi, h_lo,
                                      hi[:, ot * 512:(ot + 1) * 512], lo[:, ot * 512:(ot + 1) * 512],
                                      ic == 0, ic == ICHUNK - 1, width=512,
                                      inc=(1 if (sc == NSC - 1 and ot == 3) else 0))
                st[("tDs", ic)] = pe_last
                ws.consumed(s, pe_last)

            def down_finish_sp(ex):
                par = ex % 2
                po = par * NSC
                scat = None
                for sc in range(NSC):
                    cwcol = CWG[:, (po + sc) * NEXP + ex: (po + sc) * NEXP + ex + 1]
                    tv = None
                    for ot in range(4):
                        b = sc * 4 + ot
                        tv = p.op("vector", lambda e, b=b, sc=sc, ot=ot, cwcol=cwcol:
                                  e.tensor_scalar(A32[:, sc * 2048 + ot * 512: sc * 2048 + (ot + 1) * 512],
                                                  PB[b][:, 0:512], cwcol, None,
                                                  AluOpType.mult, AluOpType.bypass),
                                  after=[st[("tDs", ICHUNK - 1)], st[("cw_gth", par)], st.get("scat_prev")], inc=1)
                        bank_evac[b] = [tv]
                    scat = p.dma_op(
                        lambda e, sc=sc, po=po: e.indirect_dma_start(
                            out=routedtm,
                            out_offset=bass.IndirectOffsetOnAxis(SCATU[:, po + sc:po + sc + 1], 0),
                            in_=A32[:, sc * 2048:(sc + 1) * 2048], in_offset=None,
                            compute_op=AluOpType.add),
                        after=[tv, st[("scatu_tok", par)], st["rtm_zero"], st.get("scat_prev")],
                        sem="gd", eng="gpsimd")
                st["scat_prev"] = scat

            def load_back_sp():
                lb = None
                prev_tr = []
                for tc in range(4):
                    tld = p.dma_op(lambda e, tc=tc: e.dma_start(
                        out=A32[:, 4096:6144], in_=routedtm[tc * 128:(tc + 1) * 128, :]),
                        after=[st["scat_prev"]] + prev_tr, sem="dma", fence=True)
                    prev_tr = []
                    for j in range(NCHUNK):
                        b = (j % 2) * 2 + (tc % 2)
                        ttr = p.op("tensor", lambda e, j=j, b=b:
                                   e.transpose(PB[b][:, 0:128], A32[:, 4096 + j * 128: 4096 + (j + 1) * 128],
                                               EYE[:, :]),
                                   after=[tld] + bank_evac[b], inc=1)
                        prev_tr.append(ttr)
                        lb = p.op("vector", lambda e, j=j, tc=tc, b=b:
                                  e.tensor_tensor(F32B[:, j * TPASS + tc * 128: j * TPASS + (tc + 1) * 128],
                                                  PB[b][:, 0:128],
                                                  F32B[:, j * TPASS + tc * 128: j * TPASS + (tc + 1) * 128],
                                                  AluOpType.add),
                                  after=[ttr], inc=1)
                        bank_evac[b] = [lb]
                carry["a32_free"] = prev_tr[-1]
                st["routed_last"] = lb

            if SPARSE:
                st["rtm_zero"] = p.dma_op(
                    lambda e: e.dma_start(out=routedtm[0:TPASS, :], in_=zeros4m),
                    after=p.dma_totals(), sem="dma", fence=True)
            st["xk"] = build_mix(MH1, ML1, "tmk", "c2k", extra=[carry["pe"]])
            st["xv"] = build_mix(MH2, ML2, "tmv", "c2v")
            st["h_toks"] = [None] * ICHUNK

            for og in range(8):
                pairK = (0, 1) if og % 2 == 0 else (4, 5)
                pairV = (2, 3) if og % 2 == 0 else (6, 7)
                add_job(og_units(wkT[og]),
                        lambda sg, og=og, pr=pairK: st.__setitem__(("tK", og), mms_og(sg, pr, MH1, ML1, st["xk"])))

                def ewV(og=og, pairK=pairK, pairV=pairV):
                    if og == 7:
                        st["xr"] = build_mix(MH1, ML1, "tmr", "c2r", extra=[st[("tK", 7)]])
                    kv_ew(og, pairK, pairV)
                add_job(og_units(wvT[og]),
                        lambda sg, og=og, pr=pairV: st.__setitem__(("tV", og), mms_og(sg, pr, MH2, ML2, st["xv"])),
                        ewV)
            for og in range(8):
                pair = (0, 1) if og % 2 == 0 else (2, 3)

                def ewR(og=og, pair=pair):
                    r_ew(og, pair)
                    if og == 7:
                        do_fixup()
                add_job(og_units(wrT[og]),
                        lambda sg, og=og, pr=pair: st.__setitem__(("tR", og), mms_og(sg, pr, MH1, ML1, st["xr"])),
                        ewR)
            st["res_toks"] = []
            for og in range(8):
                pair = (0, 1) if og % 2 == 0 else (2, 3)

                def mmO(sg, og=og, pr=pair):
                    st[("tO", og)] = mms_og(sg, pr, MH2, ML2, [st["fix"]])

                def ewO(og=og, pair=pair):
                    for oh in range(2):
                        oc = og * 2 + oh
                        b = pair[oh]
                        t0 = p.op("vector", lambda e, b=b, oc=oc:
                                  e.tensor_tensor(chunk(A32, oc), PB[b][:, :], chunk(A32, oc), AluOpType.add),
                                  after=[st[("tO", og)]], inc=1)
                        st["res_toks"].append(t0)
                        bank_evac[b] = [t0]
                    if og == 7:
                        ln1_routing()
                add_job(og_units(woT[og]), mmO, ewO)

            if not attn_only and not SPARSE:
                for exi in range(NEXP + 1):
                    ex = exi - 1
                    gsrc = sgT if ex < 0 else gT[ex]
                    usrc = suT if ex < 0 else uT[ex]
                    dsrc = sdT if ex < 0 else dT[ex]
                    for og in range(4):
                        pairG = (0, 1) if og % 2 == 0 else (2, 3)
                        pairU = (4, 5) if og % 2 == 0 else (6, 7)
                        add_job(og_units(gsrc[og]),
                                lambda sg, og=og, pr=pairG: st.__setitem__(("tG", og), mms_og(sg, pr, MH1, ML1, st["y1"])))
                        add_job(og_units(usrc[og]),
                                lambda sg, og=og, pr=pairU: st.__setitem__(("tU", og), mms_og(sg, pr, MH1, ML1, st["y1"])),
                                lambda og=og, pG=pairG, pU=pairU, ex=ex:
                                silu_ew(og, pG, pU, None if ex < 0 else st["cw_cur"]))
                    for og in range(8):
                        pair = (0, 1) if og % 2 == 0 else (2, 3)

                        def ewD(og=og, pair=pair, ex=ex):
                            routed_evac(og, pair, first=(ex < 0))
                            if og == 7:
                                if ex < 0:
                                    carry["a32_free"] = st["routed_last"]
                                if ex < NEXP - 1:
                                    st["cw_cur"] = prep_cw(ex + 1)
                                else:
                                    ln2_stores()
                        add_job([(dsrc[og], 2048)],
                                lambda sg, og=og, pr=pair: st.__setitem__(("tD", og), mms_down(sg, pr, st["h_toks"])),
                                ewD)

            if not attn_only and SPARSE:
                for og in range(4):
                    pairG = (0, 1) if og % 2 == 0 else (2, 3)
                    pairU = (4, 5) if og % 2 == 0 else (6, 7)
                    add_job(og_units(sgT[og]),
                            lambda sg, og=og, pr=pairG: st.__setitem__(("tG", og), mms_og(sg, pr, MH1, ML1, st["y1"])))
                    add_job(og_units(suT[og]),
                            lambda sg, og=og, pr=pairU: st.__setitem__(("tU", og), mms_og(sg, pr, MH1, ML1, st["y1"])),
                            lambda og=og: silu_ew(og, (0, 1) if og % 2 == 0 else (2, 3),
                                                  (4, 5) if og % 2 == 0 else (6, 7), None))
                for og in range(8):
                    pair = (0, 1) if og % 2 == 0 else (2, 3)

                    def ewDsh(og=og, pair=pair):
                        routed_evac(og, pair, first=True)
                        if og == 2:
                            st["gf_free"] = [st[("tU", 3)]]
                            prep_sparse(0)
                    add_job([(sdT[og], 2048)],
                            lambda sg, og=og, pr=pair: st.__setitem__(("tD", og), mms_down(sg, pr, st["h_toks"])),
                            ewDsh)
                for ex in range(NEXP):
                    for og in range(4):
                        pairG = (0, 1) if og % 2 == 0 else (2, 3)
                        pairU = (4, 5) if og % 2 == 0 else (6, 7)
                        add_job(og_units(gT[ex, og]),
                                lambda sg, og=og, pr=pairG:
                                st.__setitem__(("tG", og), mms_og(sg, pr, GFH, GFL, [st["gth_tok"]], width=CAP)))
                        add_job(og_units(uT[ex, og]),
                                lambda sg, og=og, pr=pairU:
                                st.__setitem__(("tU", og), mms_og(sg, pr, GFH, GFL, [st["gth_tok"]], width=CAP)),
                                lambda og=og: silu_ew_sp(og))
                    for ic in range(ICHUNK):
                        def ewDs(ic=ic, ex=ex):
                            if ic == 2 and ex < NEXP - 1:
                                st["gf_free"] = [st[("tU", 3)]]
                                prep_sparse(ex + 1)
                            if ic == ICHUNK - 1:
                                down_finish_sp(ex)
                                if ex == NEXP - 1:
                                    load_back_sp()
                                    ln2_stores()
                        add_job([(dTm[ex, ic], 2048)],
                                lambda sg, ic=ic: mm_down_sp(sg, ic),
                                ewDs)

            staged_next = [ws.stage(a, width=w) for (a, w) in jobs[0][0]]
            for i, (units, mm_fn, ew_fn) in enumerate(jobs):
                sg = staged_next
                mm_fn(sg)
                if i + 1 < len(jobs):
                    staged_next = [ws.stage(a, width=w) for (a, w) in jobs[i + 1][0]]
                if ew_fn is not None:
                    ew_fn()

            carry["pe"] = ("tensor", p.count["tensor"])
            carry["stores"] = ("out", p.dmac["out"])

        p.barrier()
        p.op("sync", lambda e: e.nop())

        with nc.Block() as block:
            @block.sync
            def _(eng):
                p.emit("sync", eng, sems)

            @block.scalar
            def _(eng):
                p.emit("scalar", eng, sems)

            @block.vector
            def _(eng):
                p.emit("vector", eng, sems)

            @block.tensor
            def _(eng):
                p.emit("tensor", eng, sems)

            @block.gpsimd
            def _(eng):
                p.emit("gpsimd", eng, sems)

    nc._prog_stats = {e: len(p.streams[e]) for e in ENGINES}
    return nc



def _tile_proj(w):
    O, Cin = w.shape
    n_og = O // 256
    cch = Cin // 128
    wt = np.ascontiguousarray(np.asarray(w, np.float32).T)
    wt = wt.reshape(cch, 128, n_og, 256)
    wt = wt.transpose(2, 1, 0, 3).reshape(n_og, 128, cch * 256)
    return np.ascontiguousarray(wt)


def _vec_fm(v):
    return np.ascontiguousarray(np.asarray(v, np.float32).reshape(-1)[:C].reshape(NCHUNK, 128).T)


def kernel(L0_attn_x, L0_attn_time_first, L0_attn_time_mix_k, L0_attn_time_mix_v,
           L0_attn_time_mix_r, L0_attn_xx, L0_attn_aa, L0_attn_bb, L0_attn_pp,
           L0_attn_w_key, L0_attn_w_value, L0_attn_w_rec, L0_attn_w_out,
           L0_ffn_gate_w, L0_ffn_expert_bias, L0_ffn_w_gate, L0_ffn_w_up,
           L0_ffn_w_down, L0_ffn_sw_gate, L0_ffn_sw_up, L0_ffn_sw_down):
    n_cores = 8
    B, T, _ = L0_attn_x.shape
    x2d = np.ascontiguousarray(np.asarray(L0_attn_x, np.float32).reshape(B * T, C))

    common = dict(
        tmk=_vec_fm(L0_attn_time_mix_k), tmv=_vec_fm(L0_attn_time_mix_v),
        tmr=_vec_fm(L0_attn_time_mix_r), xxv=_vec_fm(L0_attn_xx),
        tfv=_vec_fm(L0_attn_time_first),
        bias_col=np.ascontiguousarray(np.asarray(L0_ffn_expert_bias, np.float32).reshape(NEXP, 1)),
        bias_b=np.ascontiguousarray(np.broadcast_to(
            np.asarray(L0_ffn_expert_bias, np.float32), (128, NEXP))),
        eye=np.eye(128, dtype=np.float32),
        wkT=_tile_proj(L0_attn_w_key), wvT=_tile_proj(L0_attn_w_value),
        wrT=_tile_proj(L0_attn_w_rec), woT=_tile_proj(L0_attn_w_out),
        gT=np.stack([_tile_proj(L0_ffn_w_gate[e]) for e in range(NEXP)]),
        uT=np.stack([_tile_proj(L0_ffn_w_up[e]) for e in range(NEXP)]),
        dT=np.stack([_tile_proj(L0_ffn_w_down[e]) for e in range(NEXP)]),
        sgT=_tile_proj(L0_ffn_sw_gate), suT=_tile_proj(L0_ffn_sw_up),
        sdT=_tile_proj(L0_ffn_sw_down),
        gwT=np.ascontiguousarray(
            np.asarray(L0_ffn_gate_w, np.float32).T.reshape(NCHUNK, 128, NEXP)
            .transpose(1, 0, 2).reshape(128, NCHUNK * NEXP)),
        dTm=np.ascontiguousarray(np.asarray(L0_ffn_w_down, np.float32)
                                 .transpose(0, 2, 1).reshape(NEXP, 8, 128, 2048)),
        zeros4m=np.zeros((TPASS, C), np.float32),
        fill512=np.full((NEXP * CAP, 1), TPASS, np.uint32),
    )
    aa, bb, pp = _vec_fm(L0_attn_aa), _vec_fm(L0_attn_bb), _vec_fm(L0_attn_pp)
    zz = np.zeros_like(aa)

    in_maps = []
    for i in range(n_cores):
        rows = slice(i * ROWS_PER_CORE, (i + 1) * ROWS_PER_CORE)
        first_is_t0 = ((i * ROWS_PER_CORE) % T) == 0
        m = dict(common)
        m["x_fm"] = np.ascontiguousarray(x2d[rows].T)
        m["aav"] = aa if first_is_t0 else zz
        m["bbv"] = bb if first_is_t0 else zz
        m["ppv"] = pp if first_is_t0 else zz
        in_maps.append(m)

    nc = build_nc()
    res = run_bass_kernel_spmd(nc, in_maps, core_ids=list(range(n_cores)))
    out = np.empty((B * T, C), dtype=np.float32)
    for i in range(n_cores):
        rows = slice(i * ROWS_PER_CORE, (i + 1) * ROWS_PER_CORE)
        out[rows] = np.asarray(res.results[i]["y_fm"]).T
    return out.reshape(B, T, C)
